# version 09 (final)
# baseline (speedup 1.0000x reference)
import math
from contextlib import ExitStack

import numpy as np

import concourse.bass as bass
import concourse.mybir as mybir
from concourse.bass_utils import run_bass_kernel_spmd

F32 = mybir.dt.float32
BF16 = mybir.dt.bfloat16
AF = mybir.ActivationFunctionType
ALU = mybir.AluOpType
AX = mybir.AxisListType

W = 8
T = 8192
C = 2048
NH = 16
HS = 128
NG = 4
TLOC = 1024
NT = 8
SCALE = 1.0 / math.sqrt(HS)
MSHIFT = -20.0
C32 = 12582912.0
F77 = 1.5 * (2.0 ** 77)
EPSV = 1e-5


def _own_blocks(c):
    return (c, 15 - c)


def _owner_loc(b):
    return (b, 0) if b <= 7 else (15 - b, 512)


class Ctr:
    __slots__ = ("h", "n")

    def __init__(self, h):
        self.h = h
        self.n = 0


class Sched:

    def __init__(self):
        self.last = {}
        self.ready = {}
        self.readers = {}

    def snap(self):
        return (dict(self.last), dict(self.ready),
                {k: list(v) for k, v in self.readers.items()})

    def restore(self, s):
        self.last = dict(s[0])
        self.ready = dict(s[1])
        self.readers = {k: list(v) for k, v in s[2].items()}

    def wt(self, eng, ctr, val=None):
        if val is None:
            val = ctr.n
        if val <= 0:
            return
        key = (id(eng), id(ctr))
        if self.last.get(key, 0) >= val:
            return
        eng.wait_ge(ctr.h, val)
        self.last[key] = val

    def produce(self, key, ctr, val=None):
        self.ready[key] = (ctr, ctr.n if val is None else val)
        self.readers[key] = []

    def consume(self, eng, key):
        c, v = self.ready[key]
        self.wt(eng, c, v)

    def note_read(self, key, ctr, val=None):
        self.readers.setdefault(key, []).append((ctr, ctr.n if val is None else val))

    def acquire(self, eng, key):
        for c, v in self.readers.get(key, []):
            self.wt(eng, c, v)
        self.readers[key] = []


def inc(instr, ctr, k=1):
    instr.then_inc(ctr.h, k)
    ctr.n += k
    return ctr.n


def build(no_switch=False, no_cc=False, cc_groups=(0, 1, 2, 3), no_dmat=False, max_phase=6, body_local_src=False, body_mode='full', body_heads=16, body_konly=False, body_noccw=False, body_nodma=False, dump=()):
    nc = bass.Bass("TRN2", target_bir_lowering=False)
    xin_d = nc.dram_tensor("x_sh", [TLOC, C], F32, kind="ExternalInput").ap()
    wat_d = nc.dram_tensor("w_attnT", [C, 3 * C], F32, kind="ExternalInput").ap()
    wpt_d = nc.dram_tensor("w_projT", [C, C], F32, kind="ExternalInput").ap()
    knt_d = nc.dram_tensor("kernelT", [C, C], F32, kind="ExternalInput").ap()
    bia_d = nc.dram_tensor("bias", [1, C], F32, kind="ExternalInput").ap()
    out_d = nc.dram_tensor("out_sh", [TLOC, C], F32, kind="ExternalOutput").ap()
    kvstg = [nc.dram_tensor(f"kvstg{g}", [1024, 1024], BF16, kind="Internal")
             for g in range(NG)]
    gath = [nc.dram_tensor(f"gath{g}", [W * 1024, 1024], BF16, kind="Internal")
            for g in range(NG)]
    kqd = nc.dram_tensor("kqd", [C, C], BF16, kind="Internal")
    xres = nc.dram_tensor("xres", [TLOC, C], F32, kind="Internal")
    dbgp = nc.dram_tensor("dbg_proj", [TLOC, C], F32, kind="ExternalOutput")
    neutral = nc.dram_tensor("neutral", [2048, 1024], BF16, kind="Internal")

    t_, v_, s_, g_, y_ = nc.tensor, nc.vector, nc.scalar, nc.gpsimd, nc.sync

    with ExitStack() as ctx:
        def sb(name, shape, dt):
            return ctx.enter_context(nc.sbuf_tensor(name, shape, dt))

        def mkpsum(name, shape, dt):
            return ctx.enter_context(nc.psum_tensor(name, shape, dt))

        def sem(name):
            return Ctr(ctx.enter_context(nc.semaphore(name)))

        XIN = sb("XIN", [128, 2 * C], F32)
        XNT = sb("XNT", [128, 2 * C], F32)
        AA = sb("AA", [128, 4 * 8192], BF16)
        AB = sb("AB", [128, 16384], BF16)
        QF = sb("QF", [128, 16384], BF16)
        FS = sb("FS", [128, 2 * C], F32)
        KQB = sb("KQB", [128, C], BF16)
        BQ = sb("BQ", [1, C], BF16)
        ATT = sb("ATT", [128, 4 * 512], BF16)
        BCS = sb("BCS", [128, 512], F32)
        RCP = sb("RCP", [1, 512], F32)
        MASK = sb("MASK", [128, 4 * 512], BF16)
        IDN = sb("IDN", [128, 128], F32)
        ONESC = sb("ONESC", [128, 1], BF16)
        ONERF = sb("ONERF", [1, 128], F32)
        ONERB = sb("ONERB", [1, 128], BF16)
        EPS = sb("EPS", [128, 1], F32)
        MB20 = sb("MB20", [128, 1], F32)
        CMG = sb("CMG", [128, 1], F32)
        TM1 = sb("TM1", [128, 1], F32)
        WBF = sb("WBF", [128, 1024], F32)
        BNS = sb("BNS", [128, 48], F32)
        STAT = sb("STAT", [128, 192], F32)

        P = [mkpsum(f"P{i}", [128, 512], F32) for i in range(8)]

        sWL = sem("sWL")
        sIN = sem("sIN")
        sW = sem("sW")
        sKQ = sem("sKQ")
        sKQD = sem("sKQD")
        sKQ2 = sem("sKQ2")
        sSTG = sem("sSTG")
        sCC = [sem(f"sCC{g}") for g in range(NG)]
        sPE = sem("sPE")
        sV = sem("sV")
        sS = sem("sS")
        sG = sem("sG")
        sXR = sem("sXR")
        sOUT = sem("sOUT")
        bKV = sem("bKV")
        bSC = sem("bSC")
        bEX = sem("bEX")
        bMK = sem("bMK")
        bPE = sem("bPE")
        bHD = sem("bHD")
        bVV = sem("bVV")
        sDT = sem("sDT")
        sDV = sem("sDV")
        sDS = sem("sDS")
        sDY = sem("sDY")

        sch = Sched()
        S0 = FS[:, 0:C]
        S1 = FS[:, C:2 * C]

        def xin(p):
            return XIN[:, C * p:C * (p + 1)]

        def xnt(p):
            return XNT[:, C * p:C * (p + 1)]


        def _dump():
            for ctr in (sPE, sV, sS, sKQD, sXR):
                sch.wt(y_, ctr)
            for g in range(NG):
                sch.wt(y_, sCC[g], 1)
            sbufs = {"AB": AB, "QF": QF, "AA": AA, "S0": S0, "BQ": BQ,
                     "CMG": CMG, "STAT": STAT}
            drams = {"kvstg0": (kvstg[0], [1024, 1024], BF16),
                     "gath0": (gath[0], [8192, 1024], BF16),
                     "kqd": (kqd, [C, C], BF16),
                     "xres": (xres, [TLOC, C], F32)}
            nd = 0
            for name in dump:
                if name in sbufs:
                    tile = sbufs[name]
                    shp = list(tile.shape)
                    dt = tile.dtype
                    od = nc.dram_tensor(f"dbg_{name}", shp, dt,
                                        kind="ExternalOutput").ap()
                    y_.dma_start(out=od, in_=tile[:, :] if len(shp) == 2 else tile)\
                      .then_inc(sOUT.h, 16)
                    nd += 16
                elif name in drams:
                    tens, shp, dt = drams[name]
                    od = nc.dram_tensor(f"dbg_{name}", shp, dt,
                                        kind="ExternalOutput").ap()
                    y_.dma_start(out=od, in_=tens.ap()).then_inc(sOUT.h, 16)
                    nd += 16
            sOUT.n += nd

        g_.memset(IDN[:, :], 0.0)
        g_.affine_select(out=IDN[:, :], in_=IDN[:, :], compare_op=ALU.not_equal,
                         fill=1.0, base=0, pattern=[[-1, 128]], channel_multiplier=1)
        for p in range(4):
            mp = MASK[:, 512 * p:512 * (p + 1)]
            g_.memset(mp, 0.0)
            g_.affine_select(out=mp, in_=mp, compare_op=ALU.is_ge, fill=-30000.0,
                             base=-128 * p, pattern=[[1, 512]], channel_multiplier=-1)
        g_.memset(ONESC[:, :], 1.0)
        g_.memset(ONERF[:, :], 1.0)
        g_.memset(ONERB[:, :], 1.0)
        g_.memset(EPS[:, :], EPSV)
        g_.memset(MB20[:, :], MSHIFT)
        inc(g_.memset(TM1[:, :], 1.0), sG)
        for eng in (t_, v_, s_):
            sch.wt(eng, sG, 1)
        v_.tensor_scalar(STAT[:, 190:191], TM1[:, :], 2.0 ** -25, 1.0,
                         ALU.add, ALU.subtract)
        v_.drain()
        inc(v_.tensor_scalar(CMG[:, :], STAT[:, 190:191], F77, C32,
                             ALU.mult, ALU.add), sV)
        v_.drain()
        cmg_ev = sV.n

        def ln_stats(src_ap, sbase):
            b = sbase
            s1 = STAT[:, b:b + 6]
            s2 = STAT[:, b + 6:b + 12]
            s3 = STAT[:, b + 12:b + 18]
            q1 = STAT[:, b + 18:b + 24]
            q2 = STAT[:, b + 24:b + 30]
            q3 = STAT[:, b + 30:b + 36]
            meanS = STAT[:, b + 36:b + 37]
            m2S = STAT[:, b + 37:b + 38]
            msqS = STAT[:, b + 38:b + 39]
            mean = STAT[:, b + 39:b + 40]
            msq = STAT[:, b + 40:b + 41]
            tmp = STAT[:, b + 41:b + 42]
            var = STAT[:, b + 42:b + 43]
            for k in range(4):
                v_.bn_stats(BNS[:, 6 * k:6 * k + 6], src_ap[:, 512 * k:512 * k + 512])
            v_.drain()
            v_.tensor_mul(BNS[:, 24:48], BNS[:, 0:24], BNS[:, 0:24])
            v_.tensor_add(s1, BNS[:, 0:6], BNS[:, 6:12])
            v_.tensor_add(s2, BNS[:, 12:18], BNS[:, 18:24])
            v_.drain()
            v_.tensor_add(s3, s1, s2)
            v_.tensor_add(q1, BNS[:, 24:30], BNS[:, 30:36])
            v_.tensor_add(q2, BNS[:, 36:42], BNS[:, 42:48])
            v_.drain()
            v_.tensor_add(meanS, s3[:, 1:2], s3[:, 4:5])
            v_.tensor_add(m2S, s3[:, 2:3], s3[:, 5:6])
            v_.tensor_add(q3, q1, q2)
            v_.drain()
            v_.tensor_scalar(mean, meanS, 1.0 / 8, 0.0, ALU.mult, ALU.add)
            v_.tensor_add(msqS, q3[:, 1:2], q3[:, 4:5])
            v_.drain()
            v_.tensor_mul(msq, mean, mean)
            v_.scalar_tensor_tensor(tmp, msqS, 256.0, m2S, ALU.mult, ALU.add)
            v_.drain()
            v_.scalar_tensor_tensor(var, tmp, 1.0 / 2048, msq,
                                    ALU.mult, ALU.subtract)
            v_.drain()
            return mean, var

        def ln_apply(src_ap, dst_ap, sbase, mean, var):
            sd = STAT[:, sbase + 43:sbase + 44]
            rstd = STAT[:, sbase + 44:sbase + 45]
            nmb = STAT[:, sbase + 45:sbase + 46]
            inc(v_.drain(), sV)
            sch.wt(s_, sV)
            s_.activation(sd, var, AF.Sqrt, bias=EPS[:, :], scale=1.0)
            inc(s_.drain(), sS)
            sch.wt(v_, sS)
            v_.reciprocal(rstd, sd)
            v_.drain()
            v_.scalar_tensor_tensor(nmb, mean, -1.0, rstd, ALU.mult, ALU.mult)
            inc(v_.drain(), sV)
            sch.wt(s_, sV)
            return inc(s_.activation(dst_ap, src_ap, AF.Identity, bias=nmb,
                                     scale=rstd), sS)

        def quant_stats(src_ap, kb, stat_rows=None):
            st = STAT if stat_rows is None else STAT[stat_rows[0]:stat_rows[1], :]
            am = st[:, kb:kb + 1]
            amg = st[:, kb + 1:kb + 2]
            rc = st[:, kb + 2:kb + 3]
            rc127 = st[:, kb + 3:kb + 4]
            scb = st[:, kb + 4:kb + 5]
            v_.tensor_reduce(am, src_ap, AX.X, ALU.max, apply_absolute_value=True)
            v_.drain()
            v_.tensor_scalar_max(amg, am, 1e-30)
            v_.drain()
            v_.reciprocal(rc, amg)
            v_.tensor_scalar(scb, amg, 1.0 / 127, 0.0, ALU.mult, ALU.add)
            v_.drain()
            v_.tensor_scalar(rc127, rc, 127.0, 0.0, ALU.mult, ALU.add)
            v_.drain()
            return rc127, scb

        kq_pos = [0]

        def quant_chunk(i):
            sch.acquire(y_, "S0")
            inc(y_.dma_start(out=S0, in_=knt_d[128 * i:128 * (i + 1), :]), sKQ, 16)
            sch.produce("kqS0", sKQ)
            sch.consume(v_, "kqS0")
            rc127, scb = quant_stats(S0, 128 + 16 * (i % 2))
            sch.acquire(v_, "S1")
            ev = inc(v_.tensor_scalar(S1, S0, rc127, CMG[:, :], ALU.mult, ALU.add), sV)
            sch.note_read("S0", sV, ev)
            sch.acquire(v_, "KQB")
            ev = inc(v_.tensor_scalar(KQB[:, :], S1, CMG[:, :], scb,
                                      ALU.subtract, ALU.mult), sV)
            sch.note_read("S1", sV, ev)
            sch.produce("KQB", sV, ev)
            sch.consume(y_, "KQB")
            d = inc(y_.dma_start(out=kqd.ap()[128 * i:128 * (i + 1), :],
                                 in_=KQB[:, :]), sKQD, 16)
            sch.note_read("KQB", sKQD, d)

        xnf_done = 0
        for t in range(NT):
            p = t % 2
            sb_ = 48 * p
            sch.acquire(y_, f"XIN{p}")
            inc(y_.dma_start(out=xin(p), in_=xin_d[128 * t:128 * (t + 1), :]), sIN, 16)
            sch.produce(f"XIN{p}", sIN)
            sch.consume(v_, f"XIN{p}")
            sch.acquire(v_, f"STAT{p}")
            sch.wt(v_, sV, cmg_ev)
            mean, var = ln_stats(xin(p), sb_)
            sch.acquire(s_, f"XNT{p}")
            ev = ln_apply(xin(p), xnt(p), sb_, mean, var)
            sch.note_read(f"XIN{p}", sS, ev)
            sch.note_read(f"STAT{p}", sS, ev)
            sch.produce(f"XNT{p}", sS, ev)
            sch.consume(t_, f"XNT{p}")
            for cc in range(16):
                bk = P[cc % 4]
                sch.acquire(t_, f"P{cc % 4}")
                tev = inc(t_.transpose(bk[:, 0:128],
                                       xnt(p)[:, 128 * cc:128 * (cc + 1)],
                                       IDN[:, :]), sPE)
                if cc == 15:
                    sch.note_read(f"XNT{p}", sPE, tev)
                sch.wt(v_, sPE, tev)
                ev = inc(v_.tensor_copy(
                    AB[:, 1024 * cc + 128 * t:1024 * cc + 128 * (t + 1)],
                    bk[:, 0:128]), sV)
                sch.note_read(f"P{cc % 4}", sV, ev)
                xnf_done = ev
            for _ in range(2):
                if kq_pos[0] < 16:
                    quant_chunk(kq_pos[0])
                    kq_pos[0] += 1

        sch.acquire(y_, "S1")
        inc(y_.dma_start(out=FS[0:1, C:2 * C], in_=bia_d), sKQ, 16)
        sch.produce("biaS1", sKQ)
        sch.consume(v_, "biaS1")
        b1 = FS[0:1, C:2 * C]
        rc127, scb = quant_stats(b1, 160, stat_rows=(0, 1))
        v_.tensor_scalar(b1, b1, rc127, CMG[0:1, :], ALU.mult, ALU.add)
        ev = inc(v_.tensor_scalar(BQ[:, :], b1, CMG[0:1, :], scb,
                                  ALU.subtract, ALU.mult), sV)
        sch.note_read("S1", sV, ev)
        sch.produce("BQ", sV, ev)

        def wb(slot):
            return AA[:, 8192 * slot:8192 * (slot + 1)]

        def load_w_window(src_col, slot):
            sch.acquire(s_, f"WB{slot}")
            for cc in range(16):
                par = cc % 2
                wbf = WBF[:, 512 * par:512 * (par + 1)]
                sch.acquire(s_, f"WBF{par}")
                d = inc(s_.dma_start(
                    out=wbf,
                    in_=wat_d[128 * cc:128 * (cc + 1), src_col:src_col + 512]),
                    sWL, 16)
                sch.produce(f"WBF{par}", sWL, d)
                sch.consume(s_, f"WBF{par}")
                ev = inc(s_.activation(wb(slot)[:, 512 * cc:512 * (cc + 1)], wbf,
                                       AF.Identity), sW)
                sch.note_read(f"WBF{par}", sW, ev)
            sch.produce(f"WB{slot}", sW)

        KSTG = [AA[:, 2 * 8192 + 512 * i:2 * 8192 + 512 * (i + 1)] for i in range(2)]
        VSTG = [AA[:, 2 * 8192 + 1024 + 512 * i:2 * 8192 + 1024 + 512 * (i + 1)]
                for i in range(2)]
        qkv_rot = [0]

        def qkv_group(lhsT_fn, rhs_fn):
            bi = 2 + (qkv_rot[0] % 2)
            qkv_rot[0] += 1
            sch.acquire(t_, f"P{bi}")
            mm = None
            for cc in range(16):
                mm = t_.matmul(P[bi][:, 0:512], lhsT_fn(cc), rhs_fn(cc),
                               start=(cc == 0), stop=(cc == 15))
            return bi, inc(mm, sPE)

        stg_count = 0
        cc_stg_counts = {}
        wslot = [0]

        def next_wslot():
            s = wslot[0]
            wslot[0] ^= 1
            return s

        for g in range(NG):
            kslot = next_wslot()
            load_w_window(C + 512 * g, kslot)
            vslot = next_wslot()
            load_w_window(2 * C + 512 * g, vslot)
            sch.consume(t_, f"WB{kslot}")
            sch.wt(t_, sV, xnf_done)
            wb_last = 0
            for oc in range(4):
                for w2 in range(2):
                    bi, ev = qkv_group(
                        lambda cc, oc=oc, ks=kslot: wb(ks)[:, 512 * cc + 128 * oc:
                                                          512 * cc + 128 * (oc + 1)],
                        lambda cc, w2=w2: AB[:, 1024 * cc + 512 * w2:
                                             1024 * cc + 512 * (w2 + 1)])
                    wb_last = ev
                    st = (oc * 2 + w2) % 2
                    sch.acquire(s_, f"KSTG{st}")
                    sch.wt(s_, sPE, ev)
                    sev = inc(s_.activation(KSTG[st], P[bi][:, 0:512], AF.Identity),
                              sS)
                    sch.note_read(f"P{bi}", sS, sev)
                    sch.wt(y_, sS, sev)
                    d = inc(y_.dma_start(
                        out=kvstg[g].ap()[128 * oc:128 * (oc + 1),
                                          512 * w2:512 * (w2 + 1)],
                        in_=KSTG[st]), sSTG, 16)
                    stg_count += 1
                    sch.note_read(f"KSTG{st}", sSTG, d)
            sch.note_read(f"WB{kslot}", sPE, wb_last)
            sch.consume(t_, f"WB{vslot}")
            vview = kvstg[g].ap()[512:1024, :].rearrange("a (b f) -> (a b) f", b=2)
            for tt in range(8):
                bi, ev = qkv_group(
                    lambda cc, tt=tt: AB[:, 1024 * cc + 128 * tt:
                                         1024 * cc + 128 * (tt + 1)],
                    lambda cc, vs=vslot: wb(vs)[:, 512 * cc:512 * (cc + 1)])
                wb_last = ev
                st = tt % 2
                sch.acquire(v_, f"VSTG{st}")
                sch.wt(v_, sPE, ev)
                vev = inc(v_.tensor_copy(VSTG[st], P[bi][:, 0:512]), sV)
                sch.note_read(f"P{bi}", sV, vev)
                sch.wt(y_, sV, vev)
                d = inc(y_.dma_start(out=vview[128 * tt:128 * (tt + 1), :],
                                     in_=VSTG[st]), sSTG, 16)
                stg_count += 1
                sch.note_read(f"VSTG{st}", sSTG, d)
            sch.note_read(f"WB{vslot}", sPE, wb_last)
            cc_stg_counts[g] = stg_count

        qf_done = 0
        for qw in range(4):
            qslot = 2 + (qw % 2)
            if qw < 2:
                for st in range(2):
                    sch.acquire(s_, f"KSTG{st}")
                    sch.acquire(s_, f"VSTG{st}")
            load_w_window(512 * qw, qslot)
            sch.consume(t_, f"WB{qslot}")
            wb_last = 0
            for oc in range(4):
                h = 4 * qw + oc
                for w2 in range(2):
                    bi, ev = qkv_group(
                        lambda cc, oc=oc, qs=qslot: wb(qs)[:, 512 * cc + 128 * oc:
                                                           512 * cc + 128 * (oc + 1)],
                        lambda cc, w2=w2: AB[:, 1024 * cc + 512 * w2:
                                             1024 * cc + 512 * (w2 + 1)])
                    wb_last = ev
                    sch.wt(s_, sPE, ev)
                    sev = inc(s_.activation(
                        QF[:, 1024 * h + 512 * w2:1024 * h + 512 * (w2 + 1)],
                        P[bi][:, 0:512], AF.Identity), sS)
                    sch.note_read(f"P{bi}", sS, sev)
                    qf_done = sev
            sch.note_read(f"WB{qslot}", sPE, wb_last)
        qkv_pe_done = sPE.n
        for g in range(NG):
            g_.wait_ge(sSTG.h, 16 * cc_stg_counts[g])
            if no_cc or g not in cc_groups:
                g_.dma_start(out=gath[g].ap()[1024 * 0:1024 * 1, :],
                             in_=kvstg[g].ap()).then_inc(sCC[g].h, 16)
            else:
                g_.collective_compute(
                    "AllGather", ALU.bypass,
                    replica_groups=[list(range(W))],
                    ins=[kvstg[g].ap().opt()],
                    outs=[gath[g].ap().opt()],
                ).then_inc(sCC[g].h)
            sCC[g].n = 1

        if max_phase <= 2:
            _dump()
            for g in range(NG):
                y_.wait_ge(sCC[g].h, 1)
            for t in range(NT):
                inc(y_.dma_start(out=out_d[128 * t:128 * (t + 1), :], in_=xin(0)),
                    sOUT, 16)
            y_.wait_ge(sOUT.h, sOUT.n)
            return nc
        env = dict(AA=AA, AB=AB, QF=QF, ATT=ATT, BCS=BCS, RCP=RCP, MASK=MASK,
                   ONESC=ONESC, ONERF=ONERF, MB20=MB20, P=P, gath=gath, sCC=sCC,
                   sPE=sPE, sS=sS, qf_done=qf_done, qkv_pe_done=qkv_pe_done,
                   body_local_src=body_local_src, kvstg=kvstg,
                   body_mode=body_mode, neutral=neutral,
                   body_heads=body_heads, body_konly=body_konly,
                   body_noccw=body_noccw, body_nodma=body_nodma)
        pre_snap = sch.snap()
        if no_switch:
            body_iter = [0]
        else:
            pids = [eng.partition_id() for eng in (t_, v_, s_, y_)]
            body_iter = nc.Switch(engines=[t_, v_, s_, y_], index=pids, n=W)
        for c in body_iter:
            sch.restore(pre_snap)
            body = dict(bKV=Ctr(bKV.h), bSC=Ctr(bSC.h), bEX=Ctr(bEX.h),
                        bMK=Ctr(bMK.h), bPE=Ctr(bPE.h), bHD=Ctr(bHD.h),
                        bVV=Ctr(bVV.h))
            _attention_body(nc, c, sch, body, env)
            inc(t_.nop(), sDT)
            sDT.n = 1
            inc(v_.engine_nop(), sDV)
            sDV.n = 1
            s_.copy(TM1[:, :], TM1[:, :]).then_inc(sDS.h, 1)
            sDS.n = 1
            inc(y_.nop(), sDY)
            sDY.n = 1
        sch.restore(pre_snap)
        for eng in (t_, v_, s_, y_, g_):
            for d in (sDT, sDV, sDS, sDY):
                eng.wait_ge(d.h, 1)
                sch.last[(id(eng), id(d))] = 1
        sch.readers = {}

        if max_phase <= 3:
            _dump()
            for t in range(NT):
                inc(y_.dma_start(out=out_d[128 * t:128 * (t + 1), :], in_=xin(0)),
                    sOUT, 16)
            y_.wait_ge(sOUT.h, sOUT.n)
            return nc
        for cc16 in range(16):
            par = cc16 % 2
            sch.acquire(s_, f"XNT{par}")
            d = inc(s_.dma_start(out=xnt(par),
                                 in_=wpt_d[128 * cc16:128 * (cc16 + 1), :]), sWL, 16)
            sch.produce(f"XNT{par}", sWL, d)
            sch.consume(s_, f"XNT{par}")
            ev = inc(s_.activation(AA[:, 2048 * cc16:2048 * (cc16 + 1)], xnt(par),
                                   AF.Identity), sW)
            sch.note_read(f"XNT{par}", sW, ev)
        pwb0 = sW.n

        for qt in range(NT):
            p = qt % 2
            banks = [0, 1, 2, 3] if p == 0 else [4, 5, 6, 7]
            sch.acquire(y_, f"XIN{p}")
            inc(y_.dma_start(out=xin(p), in_=xin_d[128 * qt:128 * (qt + 1), :]),
                sIN, 16)
            sch.produce(f"XIN{p}", sIN)
            sch.wt(t_, sW, pwb0)
            pev = 0
            for fw in range(4):
                bi = banks[fw]
                sch.acquire(t_, f"P{bi}")
                mm = None
                for cc in range(16):
                    mm = t_.matmul(
                        P[bi][:, 0:512],
                        AB[:, 1024 * cc + 128 * qt:1024 * cc + 128 * (qt + 1)],
                        AA[:, 2048 * cc + 512 * fw:2048 * cc + 512 * (fw + 1)],
                        start=(cc == 0), stop=(cc == 15))
                pev = inc(mm, sPE)
            sch.note_read("AByfm", sPE, pev)
            sch.note_read("PWB", sPE, pev)
            sch.acquire(v_, "S0")
            sch.wt(v_, sPE, pev)
            vev = 0
            for fw in range(4):
                vev = inc(v_.tensor_copy(S0[:, 512 * fw:512 * (fw + 1)],
                                         P[banks[fw]][:, 0:512]), sV)
            for fw in range(4):
                sch.note_read(f"P{banks[fw]}", sV, vev)
            if "proj" in dump:
                sch.wt(y_, sV, vev)
                d = inc(y_.dma_start(out=dbgp.ap()[128 * qt:128 * (qt + 1), :],
                                     in_=S0), sXR, 16)
                sch.note_read("S0", sXR, d)
            sb_ = 48 * p
            sch.acquire(v_, f"STAT{p}")
            mean, var = ln_stats(S0, sb_)
            sch.acquire(s_, "S1")
            ev = ln_apply(S0, S1, sb_, mean, var)
            sch.note_read(f"STAT{p}", sS, ev)
            sch.wt(v_, sS, ev)
            sch.consume(v_, f"XIN{p}")
            vev = inc(v_.tensor_add(S0, S1, xin(p)), sV)
            sch.note_read("S1", sV, vev)
            sch.note_read(f"XIN{p}", sV, vev)
            sch.wt(y_, sV, vev)
            d = inc(y_.dma_start(out=xres.ap()[128 * qt:128 * (qt + 1), :], in_=S0),
                    sXR, 16)
            sch.note_read("S0", sXR, d)

        if max_phase <= 4:
            _dump()
            for t in range(NT):
                y_.wait_ge(sXR.h, 16 * NT)
                inc(y_.dma_start(out=out_d[128 * t:128 * (t + 1), :], in_=xin(0)),
                    sOUT, 16)
            y_.wait_ge(sOUT.h, sOUT.n)
            return nc
        xqf_tile_ev = {}
        for qt in range(NT):
            p = qt % 2
            sch.acquire(y_, f"XIN{p}")
            sch.wt(y_, sXR, (32 if "proj" in dump else 16) * (qt + 1))
            inc(y_.dma_start(out=xin(p), in_=xres.ap()[128 * qt:128 * (qt + 1), :]),
                sIN, 16)
            sch.produce(f"XIN{p}", sIN)
            sch.consume(v_, f"XIN{p}")
            sb_ = 48 * p
            sch.acquire(v_, f"STAT{p}")
            mean, var = ln_stats(xin(p), sb_)
            sch.acquire(s_, "S0")
            ev = ln_apply(xin(p), S0, sb_, mean, var)
            sch.note_read(f"XIN{p}", sS, ev)
            sch.note_read(f"STAT{p}", sS, ev)
            sch.wt(v_, sS, ev)
            rc127, scb = quant_stats(S0, 96 + 16 * p)
            sch.acquire(v_, "S1")
            ev = inc(v_.tensor_scalar(S1, S0, rc127, CMG[:, :], ALU.mult, ALU.add),
                     sV)
            sch.note_read("S0", sV, ev)
            sch.acquire(v_, f"XNT{p}")
            ev = inc(v_.tensor_scalar(xnt(p), S1, CMG[:, :], scb,
                                      ALU.subtract, ALU.mult), sV)
            sch.note_read("S1", sV, ev)
            sch.produce(f"XNT{p}", sV, ev)
            sch.consume(t_, f"XNT{p}")
            sch.acquire(t_, "AByfm") if qt == 0 else None
            for cc in range(16):
                bk = P[cc % 4]
                sch.acquire(t_, f"P{cc % 4}")
                tev = inc(t_.transpose(bk[:, 0:128],
                                       xnt(p)[:, 128 * cc:128 * (cc + 1)],
                                       IDN[:, :]), sPE)
                if cc == 15:
                    sch.note_read(f"XNT{p}", sPE, tev)
                sch.wt(v_, sPE, tev)
                ev = inc(v_.tensor_copy(
                    AB[:, 1024 * cc + 128 * qt:1024 * cc + 128 * (qt + 1)],
                    bk[:, 0:128]), sV)
                sch.note_read(f"P{cc % 4}", sV, ev)
            xqf_tile_ev[qt] = ev

        if max_phase <= 5:
            _dump()
            for t in range(NT):
                inc(y_.dma_start(out=out_d[128 * t:128 * (t + 1), :], in_=xin(0)),
                    sOUT, 16)
            y_.wait_ge(sOUT.h, sOUT.n)
            return nc
        KQT = [AA[:, 2 * 8192 + 2048 * i:2 * 8192 + 2048 * (i + 1)] for i in range(2)]
        sch.acquire(y_, "PWB")
        for tp in range(4):
            tts = (2 * tp, 2 * tp + 1)
            for tt in tts:
                sch.wt(t_, sV, xqf_tile_ev[tt])
            for bi in range(8):
                sch.acquire(t_, f"P{bi}")
            for dc in range(16):
                kp = dc % 2
                sch.acquire(y_, f"KQT{kp}")
                sch.wt(y_, sKQD, 16 * 16)
                if no_dmat:
                    d = inc(y_.dma_start(
                        out=KQT[kp], in_=kqd.ap()[128 * dc:128 * (dc + 1), :].rearrange(
                            "a (b c) -> a b c", b=1)), sKQ2, 16)
                else:
                    d = inc(y_.dma_start_transpose(
                        KQT[kp], kqd.ap()[:, 128 * dc:128 * (dc + 1)]), sKQ2, 16)
                sch.produce(f"KQT{kp}", sKQ2, d)
                sch.consume(t_, f"KQT{kp}")
                mm = None
                for ti, tt in enumerate(tts):
                    banks = [0, 1, 2, 3] if ti == 0 else [4, 5, 6, 7]
                    for fw in range(4):
                        mm = t_.matmul(
                            P[banks[fw]][:, 0:512],
                            AB[:, 1024 * dc + 128 * tt:1024 * dc + 128 * (tt + 1)],
                            KQT[kp][:, 512 * fw:512 * (fw + 1)],
                            start=(dc == 0), stop=False)
                kev = inc(mm, sPE)
                sch.note_read(f"KQT{kp}", sPE, kev)
            sch.consume(t_, "BQ")
            for ti, tt in enumerate(tts):
                banks = [0, 1, 2, 3] if ti == 0 else [4, 5, 6, 7]
                mm = None
                for fw in range(4):
                    mm = t_.matmul(P[banks[fw]][:, 0:512], ONERB[:, :],
                                   BQ[0:1, 512 * fw:512 * (fw + 1)],
                                   start=False, stop=True)
                pev = inc(mm, sPE)
                p = tt % 2
                sch.acquire(y_, f"XIN{p}")
                inc(y_.dma_start(out=xin(p),
                                 in_=xres.ap()[128 * tt:128 * (tt + 1), :]), sIN, 16)
                sch.produce(f"XIN{p}", sIN)
                sch.acquire(v_, "S0")
                sch.wt(v_, sPE, pev)
                vev = 0
                for fw in range(4):
                    vev = inc(v_.tensor_scalar_max(S0[:, 512 * fw:512 * (fw + 1)],
                                                   P[banks[fw]][:, 0:512], 0.0), sV)
                for fw in range(4):
                    sch.note_read(f"P{banks[fw]}", sV, vev)
                rc127, scb = quant_stats(S0, 96 + 16 * p)
                sch.acquire(v_, "S1")
                v_.tensor_scalar(S1, S0, rc127, CMG[:, :], ALU.mult, ALU.add)
                v_.tensor_scalar(S0, S1, CMG[:, :], scb, ALU.subtract, ALU.mult)
                sb_ = 48 * p
                sch.acquire(v_, f"STAT{p}")
                mean, var = ln_stats(S0, sb_)
                ev = ln_apply(S0, S1, sb_, mean, var)
                sch.note_read(f"STAT{p}", sS, ev)
                sch.wt(v_, sS, ev)
                sch.consume(v_, f"XIN{p}")
                vev = inc(v_.tensor_add(S0, S1, xin(p)), sV)
                sch.note_read("S1", sV, vev)
                sch.note_read(f"XIN{p}", sV, vev)
                sch.wt(y_, sV, vev)
                d = inc(y_.dma_start(out=out_d[128 * tt:128 * (tt + 1), :], in_=S0),
                        sOUT, 16)
                sch.note_read("S0", sOUT, d)
        _dump()
        y_.wait_ge(sOUT.h, sOUT.n)
    return nc


def _attention_body(nc, c, sch, body, env):
    t_, v_, s_, y_ = nc.tensor, nc.vector, nc.scalar, nc.sync
    AA, AB, QF, ATT, BCS, RCP = (env["AA"], env["AB"], env["QF"], env["ATT"],
                                 env["BCS"], env["RCP"])
    MASK, ONESC, ONERF, MB20 = env["MASK"], env["ONESC"], env["ONERF"], env["MB20"]
    P = env["P"]
    gath = env["gath"]
    sCC, sPE, sS = env["sCC"], env["sPE"], env["sS"]
    bKV, bSC, bEX, bMK, bPE, bHD, bVV = (body["bKV"], body["bSC"], body["bEX"],
                                         body["bMK"], body["bPE"], body["bHD"],
                                         body["bVV"])

    def KT(par):
        return AA[:, 8192 * par:8192 * (par + 1)]

    def VT(par):
        return AA[:, 8192 * (2 + par):8192 * (3 + par)]

    nblk = 16 - c
    sch.wt(t_, sS, env["qf_done"])
    sch.wt(y_, sPE, env["qkv_pe_done"])
    att_last_av = [0, 0, 0, 0]

    for h in range(env.get('body_heads', NH)):
        par = h % 2
        g = h // 4
        hh = h % 4
        if not env.get("body_noccw"):
            sch.wt(y_, sCC[g], 1)
        if h >= 2:
            sch.wt(y_, bHD, h - 1)
        for b in range(nblk):
            r, loc = _owner_loc(b)
            if env.get("body_local_src") == "kvstg":
                srcbase = env["kvstg"][g].ap()
                r = 0
            elif env.get("body_local_src") == "neutral":
                srcbase = env["neutral"].ap()
                r = r % 2
            else:
                srcbase = gath[g].ap()
            src = srcbase[1024 * r + 128 * hh:1024 * r + 128 * (hh + 1),
                          loc:loc + 512]
            if not env.get("body_nodma"):
                inc(y_.dma_start(out=KT(par)[:, 512 * b:512 * (b + 1)], in_=src),
                    bKV, 16)
            vv = srcbase[1024 * r + 512:1024 * r + 1024, :] \
                .rearrange("a (b f) -> (a b) f", b=2)
            src2 = vv[loc:loc + 512, 128 * hh:128 * (hh + 1)] \
                .rearrange("(j p) d -> p j d", p=128)
            dst2 = VT(par)[:, 512 * b:512 * (b + 1)].rearrange("p (j d) -> p j d",
                                                               d=128)
            if not env.get("body_konly"):
                inc(y_.dma_start(out=dst2, in_=src2), bKV, 16)
        kv_ev = bKV.n
        if env.get("body_mode") == "loads":
            sch.wt(y_, bKV, kv_ev)
            inc(y_.nop(), bHD)
            continue
        hd_mark = None
        for w in range(2):
            bw = c if w == 0 else 15 - c
            n = 4 * bw + 4
            qcol = 1024 * h + 512 * w
            av_b = 4 + w
            dn_b = 6 + w
            sch.wt(t_, bKV, kv_ev)
            scev = {}
            exev = {}
            avev = {}
            mkev = {}

            def emit_sc(ci):
                sl = ci % 4
                bk = P[sl]
                sch.acquire(t_, f"P{sl}")
                scev[ci] = inc(t_.matmul(bk[:, 0:512],
                                         KT(par)[:, 128 * ci:128 * (ci + 1)],
                                         QF[:, qcol:qcol + 512],
                                         start=True, stop=True), bSC)
                dpos = ci - (n - 4)
                if dpos >= 0:
                    sch.wt(v_, bSC, scev[ci])
                    mkev[ci] = inc(v_.tensor_add(
                        bk[:, 0:512], bk[:, 0:512],
                        MASK[:, 512 * dpos:512 * (dpos + 1)]), bMK)
                sch.wt(s_, bSC, scev[ci])
                if ci in mkev:
                    sch.wt(s_, bMK, mkev[ci])
                if att_last_av[sl]:
                    sch.wt(s_, bPE, att_last_av[sl])
                exev[ci] = inc(s_.activation(ATT[:, 512 * sl:512 * (sl + 1)],
                                             bk[:, 0:512], AF.Exp,
                                             bias=MB20[:, :], scale=SCALE), bEX)
                sch.note_read(f"P{sl}", bEX, exev[ci])

            def emit_av(ci):
                sl = ci % 4
                if ci == 0:
                    sch.acquire(t_, f"P{av_b}")
                    sch.acquire(t_, f"P{dn_b}")
                sch.wt(t_, bEX, exev[ci])
                t_.matmul(P[av_b][:, 0:512], VT(par)[:, 128 * ci:128 * (ci + 1)],
                          ATT[:, 512 * sl:512 * (sl + 1)],
                          start=(ci == 0), stop=(ci == n - 1))
                avev[ci] = inc(t_.matmul(P[dn_b][0:1, 0:512], ONESC[:, :],
                                         ATT[:, 512 * sl:512 * (sl + 1)],
                                         start=(ci == 0), stop=(ci == n - 1)), bPE)
                att_last_av[sl] = avev[ci]

            emit_sc(0)
            for ci in range(1, n):
                emit_sc(ci)
                emit_av(ci - 1)
            emit_av(n - 1)
            if w == 1:
                hd_mark = t_.nop()
            sch.wt(v_, bPE, avev[n - 1])
            sch.acquire(v_, "RCP")
            v_.reciprocal(RCP[:, :], P[dn_b][0:1, 0:512])
            rcev = inc(v_.drain(), bVV)
            sch.note_read(f"P{dn_b}", bVV, rcev)
            sch.produce("RCP", bVV, rcev)
            bsl = n % 4
            sch.acquire(t_, f"P{bsl}")
            sch.consume(t_, "RCP")
            bcev = inc(t_.matmul(P[bsl][:, 0:512], ONERF[:, :], RCP[:, :],
                                 start=True, stop=True), bPE)
            sch.wt(v_, bPE, bcev)
            sch.acquire(v_, "BCS")
            bev = inc(v_.tensor_copy(BCS[:, :], P[bsl][:, 0:512]), bVV)
            sch.note_read(f"P{bsl}", bVV, bev)
            sch.note_read("RCP", bVV, bev)
            v_.drain()
            if h == 0 and w == 0:
                sch.wt(v_, sPE, env["qkv_pe_done"])
            yev = inc(v_.tensor_mul(AB[:, qcol:qcol + 512], P[av_b][:, 0:512],
                                    BCS[:, :]), bVV)
            sch.note_read(f"P{av_b}", bVV, yev)
            sch.note_read("BCS", bVV, yev)
        inc(hd_mark, bHD)


def kernel(L0_attn_x, L0_attn_w_attn, L0_attn_w_proj, L0_ffn_kernel, L0_ffn_bias):
    x = np.ascontiguousarray(np.asarray(L0_attn_x, dtype=np.float32).reshape(T, C))
    w_attnT = np.ascontiguousarray(np.asarray(L0_attn_w_attn, dtype=np.float32).T)
    w_projT = np.ascontiguousarray(np.asarray(L0_attn_w_proj, dtype=np.float32).T)
    kernelT = np.ascontiguousarray(np.asarray(L0_ffn_kernel, dtype=np.float32).T)
    bias = np.ascontiguousarray(np.asarray(L0_ffn_bias, dtype=np.float32).reshape(1, C))

    in_maps = []
    for c in range(W):
        bA, bB = _own_blocks(c)
        xs = np.concatenate([x[512 * bA:512 * (bA + 1)],
                             x[512 * bB:512 * (bB + 1)]], axis=0)
        in_maps.append({
            "x_sh": np.ascontiguousarray(xs),
            "w_attnT": w_attnT,
            "w_projT": w_projT,
            "kernelT": kernelT,
            "bias": bias,
        })

    nc = build()
    res = run_bass_kernel_spmd(nc, in_maps, core_ids=list(range(W)))
    out = np.empty((T, C), dtype=np.float32)
    for c in range(W):
        sh = np.asarray(res.results[c]["out_sh"])
        bA, bB = _own_blocks(c)
        out[512 * bA:512 * (bA + 1)] = sh[:512]
        out[512 * bB:512 * (bB + 1)] = sh[512:]
    return out.reshape(1, T, C)


# version 10
# speedup vs baseline: 1.0343x; 1.0343x over previous
import math
from contextlib import ExitStack

import numpy as np

import concourse.bass as bass
import concourse.mybir as mybir
from concourse.bass_utils import run_bass_kernel_spmd

F32 = mybir.dt.float32
BF16 = mybir.dt.bfloat16
AF = mybir.ActivationFunctionType
ALU = mybir.AluOpType
AX = mybir.AxisListType

W = 8
T = 8192
C = 2048
NH = 16
HS = 128
NG = 4
TLOC = 1024
NT = 8
SCALE = 1.0 / math.sqrt(HS)
MSHIFT = -20.0
C32 = 12582912.0
F77 = 1.5 * (2.0 ** 77)
EPSV = 1e-5


def _own_blocks(c):
    return (c, 15 - c)


def _owner_loc(b):
    return (b, 0) if b <= 7 else (15 - b, 512)


class Ctr:
    __slots__ = ("h", "n")

    def __init__(self, h):
        self.h = h
        self.n = 0


class Sched:

    def __init__(self):
        self.last = {}
        self.ready = {}
        self.readers = {}

    def snap(self):
        return (dict(self.last), dict(self.ready),
                {k: list(v) for k, v in self.readers.items()})

    def restore(self, s):
        self.last = dict(s[0])
        self.ready = dict(s[1])
        self.readers = {k: list(v) for k, v in s[2].items()}

    def wt(self, eng, ctr, val=None):
        if val is None:
            val = ctr.n
        if val <= 0:
            return
        key = (id(eng), id(ctr))
        if self.last.get(key, 0) >= val:
            return
        eng.wait_ge(ctr.h, val)
        self.last[key] = val

    def produce(self, key, ctr, val=None):
        self.ready[key] = (ctr, ctr.n if val is None else val)
        self.readers[key] = []

    def consume(self, eng, key):
        c, v = self.ready[key]
        self.wt(eng, c, v)

    def note_read(self, key, ctr, val=None):
        self.readers.setdefault(key, []).append((ctr, ctr.n if val is None else val))

    def acquire(self, eng, key):
        for c, v in self.readers.get(key, []):
            self.wt(eng, c, v)
        self.readers[key] = []


def inc(instr, ctr, k=1):
    instr.then_inc(ctr.h, k)
    ctr.n += k
    return ctr.n


def build(no_switch=False, no_cc=False, cc_groups=(0, 1, 2, 3), no_dmat=False, max_phase=6, body_local_src=False, body_mode='full', body_heads=16, body_konly=False, body_noccw=False, body_nodma=False, dump=()):
    nc = bass.Bass("TRN2", target_bir_lowering=False)
    xin_d = nc.dram_tensor("x_sh", [TLOC, C], F32, kind="ExternalInput").ap()
    wat_d = nc.dram_tensor("w_attnT", [C, 3 * C], F32, kind="ExternalInput").ap()
    wpt_d = nc.dram_tensor("w_projT", [C, C], F32, kind="ExternalInput").ap()
    knt_d = nc.dram_tensor("kernelT", [C, C], F32, kind="ExternalInput").ap()
    bia_d = nc.dram_tensor("bias", [1, C], F32, kind="ExternalInput").ap()
    out_d = nc.dram_tensor("out_sh", [TLOC, C], F32, kind="ExternalOutput").ap()
    kvstg = [nc.dram_tensor(f"kvstg{g}", [1024, 1024], BF16, kind="Internal")
             for g in range(NG)]
    gath = [nc.dram_tensor(f"gath{g}", [W * 1024, 1024], BF16, kind="Internal")
            for g in range(NG)]
    kqd = nc.dram_tensor("kqd", [C, C], BF16, kind="Internal")
    xres = nc.dram_tensor("xres", [TLOC, C], F32, kind="Internal")
    dbgp = nc.dram_tensor("dbg_proj", [TLOC, C], F32, kind="ExternalOutput")
    neutral = nc.dram_tensor("neutral", [2048, 1024], BF16, kind="Internal")

    t_, v_, s_, g_, y_ = nc.tensor, nc.vector, nc.scalar, nc.gpsimd, nc.sync

    with ExitStack() as ctx:
        def sb(name, shape, dt):
            return ctx.enter_context(nc.sbuf_tensor(name, shape, dt))

        def mkpsum(name, shape, dt):
            return ctx.enter_context(nc.psum_tensor(name, shape, dt))

        def sem(name):
            return Ctr(ctx.enter_context(nc.semaphore(name)))

        XIN = sb("XIN", [128, 2 * C], F32)
        XNT = sb("XNT", [128, 2 * C], F32)
        AA = sb("AA", [128, 4 * 8192], BF16)
        AB = sb("AB", [128, 16384], BF16)
        QF = sb("QF", [128, 16384], BF16)
        FS = sb("FS", [128, 2 * C], F32)
        KQB = sb("KQB", [128, C], BF16)
        BQ = sb("BQ", [1, C], BF16)
        ATT = sb("ATT", [128, 4 * 512], BF16)
        BCS = sb("BCS", [128, 512], F32)
        RCP = sb("RCP", [1, 512], F32)
        MASK = sb("MASK", [128, 4 * 512], BF16)
        IDN = sb("IDN", [128, 128], F32)
        ONESC = sb("ONESC", [128, 1], BF16)
        ONERF = sb("ONERF", [1, 128], F32)
        ONERB = sb("ONERB", [1, 128], BF16)
        EPS = sb("EPS", [128, 1], F32)
        MB20 = sb("MB20", [128, 1], F32)
        CMG = sb("CMG", [128, 1], F32)
        TM1 = sb("TM1", [128, 1], F32)
        WBF = sb("WBF", [128, 1024], F32)
        BNS = sb("BNS", [128, 48], F32)
        STAT = sb("STAT", [128, 192], F32)

        P = [mkpsum(f"P{i}", [128, 512], F32) for i in range(8)]

        sWL = sem("sWL")
        sIN = sem("sIN")
        sW = sem("sW")
        sKQ = sem("sKQ")
        sKQD = sem("sKQD")
        sKQ2 = sem("sKQ2")
        sSTG = sem("sSTG")
        sCC = [sem(f"sCC{g}") for g in range(NG)]
        sPE = sem("sPE")
        sV = sem("sV")
        sS = sem("sS")
        sG = sem("sG")
        sXR = sem("sXR")
        sOUT = sem("sOUT")
        bKV = sem("bKV")
        bSC = sem("bSC")
        bEX = sem("bEX")
        bMK = sem("bMK")
        bPE = sem("bPE")
        bHD = sem("bHD")
        bVV = sem("bVV")
        sDT = sem("sDT")
        sDV = sem("sDV")
        sDS = sem("sDS")
        sDY = sem("sDY")

        sch = Sched()
        S0 = FS[:, 0:C]
        S1 = FS[:, C:2 * C]

        def xin(p):
            return XIN[:, C * p:C * (p + 1)]

        def xnt(p):
            return XNT[:, C * p:C * (p + 1)]


        def _dump():
            for ctr in (sPE, sV, sS, sKQD, sXR):
                sch.wt(y_, ctr)
            for g in range(NG):
                sch.wt(y_, sCC[g], 1)
            sbufs = {"AB": AB, "QF": QF, "AA": AA, "S0": S0, "BQ": BQ,
                     "CMG": CMG, "STAT": STAT}
            drams = {"kvstg0": (kvstg[0], [1024, 1024], BF16),
                     "gath0": (gath[0], [8192, 1024], BF16),
                     "kqd": (kqd, [C, C], BF16),
                     "xres": (xres, [TLOC, C], F32)}
            nd = 0
            for name in dump:
                if name in sbufs:
                    tile = sbufs[name]
                    shp = list(tile.shape)
                    dt = tile.dtype
                    od = nc.dram_tensor(f"dbg_{name}", shp, dt,
                                        kind="ExternalOutput").ap()
                    y_.dma_start(out=od, in_=tile[:, :] if len(shp) == 2 else tile)\
                      .then_inc(sOUT.h, 16)
                    nd += 16
                elif name in drams:
                    tens, shp, dt = drams[name]
                    od = nc.dram_tensor(f"dbg_{name}", shp, dt,
                                        kind="ExternalOutput").ap()
                    y_.dma_start(out=od, in_=tens.ap()).then_inc(sOUT.h, 16)
                    nd += 16
            sOUT.n += nd

        g_.memset(IDN[:, :], 0.0)
        g_.affine_select(out=IDN[:, :], in_=IDN[:, :], compare_op=ALU.not_equal,
                         fill=1.0, base=0, pattern=[[-1, 128]], channel_multiplier=1)
        for p in range(4):
            mp = MASK[:, 512 * p:512 * (p + 1)]
            g_.memset(mp, 0.0)
            g_.affine_select(out=mp, in_=mp, compare_op=ALU.is_ge, fill=-30000.0,
                             base=-128 * p, pattern=[[1, 512]], channel_multiplier=-1)
        g_.memset(ONESC[:, :], 1.0)
        g_.memset(ONERF[:, :], 1.0)
        g_.memset(ONERB[:, :], 1.0)
        g_.memset(EPS[:, :], EPSV)
        g_.memset(MB20[:, :], MSHIFT)
        inc(g_.memset(TM1[:, :], 1.0), sG)
        for eng in (t_, v_, s_):
            sch.wt(eng, sG, 1)
        v_.tensor_scalar(STAT[:, 190:191], TM1[:, :], 2.0 ** -25, 1.0,
                         ALU.add, ALU.subtract)
        v_.drain()
        inc(v_.tensor_scalar(CMG[:, :], STAT[:, 190:191], F77, C32,
                             ALU.mult, ALU.add), sV)
        v_.drain()
        cmg_ev = sV.n

        def ln_stats(src_ap, sbase):
            b = sbase
            s1 = STAT[:, b:b + 6]
            s2 = STAT[:, b + 6:b + 12]
            s3 = STAT[:, b + 12:b + 18]
            q1 = STAT[:, b + 18:b + 24]
            q2 = STAT[:, b + 24:b + 30]
            q3 = STAT[:, b + 30:b + 36]
            meanS = STAT[:, b + 36:b + 37]
            m2S = STAT[:, b + 37:b + 38]
            msqS = STAT[:, b + 38:b + 39]
            mean = STAT[:, b + 39:b + 40]
            msq = STAT[:, b + 40:b + 41]
            tmp = STAT[:, b + 41:b + 42]
            var = STAT[:, b + 42:b + 43]
            for k in range(4):
                v_.bn_stats(BNS[:, 6 * k:6 * k + 6], src_ap[:, 512 * k:512 * k + 512])
            v_.drain()
            v_.tensor_mul(BNS[:, 24:48], BNS[:, 0:24], BNS[:, 0:24])
            v_.tensor_add(s1, BNS[:, 0:6], BNS[:, 6:12])
            v_.tensor_add(s2, BNS[:, 12:18], BNS[:, 18:24])
            v_.drain()
            v_.tensor_add(s3, s1, s2)
            v_.tensor_add(q1, BNS[:, 24:30], BNS[:, 30:36])
            v_.tensor_add(q2, BNS[:, 36:42], BNS[:, 42:48])
            v_.drain()
            v_.tensor_add(meanS, s3[:, 1:2], s3[:, 4:5])
            v_.tensor_add(m2S, s3[:, 2:3], s3[:, 5:6])
            v_.tensor_add(q3, q1, q2)
            v_.drain()
            v_.tensor_scalar(mean, meanS, 1.0 / 8, 0.0, ALU.mult, ALU.add)
            v_.tensor_add(msqS, q3[:, 1:2], q3[:, 4:5])
            v_.drain()
            v_.tensor_mul(msq, mean, mean)
            v_.scalar_tensor_tensor(tmp, msqS, 256.0, m2S, ALU.mult, ALU.add)
            v_.drain()
            v_.scalar_tensor_tensor(var, tmp, 1.0 / 2048, msq,
                                    ALU.mult, ALU.subtract)
            v_.drain()
            return mean, var

        def ln_apply(src_ap, dst_ap, sbase, mean, var):
            sd = STAT[:, sbase + 43:sbase + 44]
            rstd = STAT[:, sbase + 44:sbase + 45]
            nmb = STAT[:, sbase + 45:sbase + 46]
            inc(v_.drain(), sV)
            sch.wt(s_, sV)
            s_.activation(sd, var, AF.Sqrt, bias=EPS[:, :], scale=1.0)
            inc(s_.drain(), sS)
            sch.wt(v_, sS)
            v_.reciprocal(rstd, sd)
            v_.drain()
            v_.scalar_tensor_tensor(nmb, mean, -1.0, rstd, ALU.mult, ALU.mult)
            inc(v_.drain(), sV)
            sch.wt(s_, sV)
            return inc(s_.activation(dst_ap, src_ap, AF.Identity, bias=nmb,
                                     scale=rstd), sS)

        def quant_stats(src_ap, kb, stat_rows=None):
            st = STAT if stat_rows is None else STAT[stat_rows[0]:stat_rows[1], :]
            am = st[:, kb:kb + 1]
            amg = st[:, kb + 1:kb + 2]
            rc = st[:, kb + 2:kb + 3]
            rc127 = st[:, kb + 3:kb + 4]
            scb = st[:, kb + 4:kb + 5]
            v_.tensor_reduce(am, src_ap, AX.X, ALU.max, apply_absolute_value=True)
            v_.drain()
            v_.tensor_scalar_max(amg, am, 1e-30)
            v_.drain()
            v_.reciprocal(rc, amg)
            v_.tensor_scalar(scb, amg, 1.0 / 127, 0.0, ALU.mult, ALU.add)
            v_.drain()
            v_.tensor_scalar(rc127, rc, 127.0, 0.0, ALU.mult, ALU.add)
            v_.drain()
            return rc127, scb

        kq_pos = [0]

        def quant_chunk(i):
            sch.acquire(y_, "S0")
            inc(y_.dma_start(out=S0, in_=knt_d[128 * i:128 * (i + 1), :]), sKQ, 16)
            sch.produce("kqS0", sKQ)
            sch.consume(v_, "kqS0")
            rc127, scb = quant_stats(S0, 128 + 16 * (i % 2))
            sch.acquire(v_, "S1")
            ev = inc(v_.tensor_scalar(S1, S0, rc127, CMG[:, :], ALU.mult, ALU.add), sV)
            sch.note_read("S0", sV, ev)
            sch.acquire(v_, "KQB")
            ev = inc(v_.tensor_scalar(KQB[:, :], S1, CMG[:, :], scb,
                                      ALU.subtract, ALU.mult), sV)
            sch.note_read("S1", sV, ev)
            sch.produce("KQB", sV, ev)
            sch.consume(y_, "KQB")
            d = inc(y_.dma_start(out=kqd.ap()[128 * i:128 * (i + 1), :],
                                 in_=KQB[:, :]), sKQD, 16)
            sch.note_read("KQB", sKQD, d)

        xnf_done = 0
        for t in range(NT):
            p = t % 2
            sb_ = 48 * p
            sch.acquire(y_, f"XIN{p}")
            inc(y_.dma_start(out=xin(p), in_=xin_d[128 * t:128 * (t + 1), :]), sIN, 16)
            sch.produce(f"XIN{p}", sIN)
            sch.consume(v_, f"XIN{p}")
            sch.acquire(v_, f"STAT{p}")
            sch.wt(v_, sV, cmg_ev)
            mean, var = ln_stats(xin(p), sb_)
            sch.acquire(s_, f"XNT{p}")
            ev = ln_apply(xin(p), xnt(p), sb_, mean, var)
            sch.note_read(f"XIN{p}", sS, ev)
            sch.note_read(f"STAT{p}", sS, ev)
            sch.produce(f"XNT{p}", sS, ev)
            sch.consume(t_, f"XNT{p}")
            for cc in range(16):
                bk = P[cc % 4]
                sch.acquire(t_, f"P{cc % 4}")
                tev = inc(t_.transpose(bk[:, 0:128],
                                       xnt(p)[:, 128 * cc:128 * (cc + 1)],
                                       IDN[:, :]), sPE)
                if cc == 15:
                    sch.note_read(f"XNT{p}", sPE, tev)
                sch.wt(v_, sPE, tev)
                ev = inc(v_.tensor_copy(
                    AB[:, 1024 * cc + 128 * t:1024 * cc + 128 * (t + 1)],
                    bk[:, 0:128]), sV)
                sch.note_read(f"P{cc % 4}", sV, ev)
                xnf_done = ev
            for _ in range(2):
                if kq_pos[0] < 16:
                    quant_chunk(kq_pos[0])
                    kq_pos[0] += 1

        sch.acquire(y_, "S1")
        inc(y_.dma_start(out=FS[0:1, C:2 * C], in_=bia_d), sKQ, 16)
        sch.produce("biaS1", sKQ)
        sch.consume(v_, "biaS1")
        b1 = FS[0:1, C:2 * C]
        rc127, scb = quant_stats(b1, 160, stat_rows=(0, 1))
        v_.tensor_scalar(b1, b1, rc127, CMG[0:1, :], ALU.mult, ALU.add)
        ev = inc(v_.tensor_scalar(BQ[:, :], b1, CMG[0:1, :], scb,
                                  ALU.subtract, ALU.mult), sV)
        sch.note_read("S1", sV, ev)
        sch.produce("BQ", sV, ev)

        def wb(slot):
            return AA[:, 8192 * slot:8192 * (slot + 1)]

        def load_w_window(src_col, slot):
            sch.acquire(s_, f"WB{slot}")
            for cc in range(16):
                par = cc % 2
                wbf = WBF[:, 512 * par:512 * (par + 1)]
                sch.acquire(s_, f"WBF{par}")
                d = inc(s_.dma_start(
                    out=wbf,
                    in_=wat_d[128 * cc:128 * (cc + 1), src_col:src_col + 512]),
                    sWL, 16)
                sch.produce(f"WBF{par}", sWL, d)
                sch.consume(s_, f"WBF{par}")
                ev = inc(s_.activation(wb(slot)[:, 512 * cc:512 * (cc + 1)], wbf,
                                       AF.Identity), sW)
                sch.note_read(f"WBF{par}", sW, ev)
            sch.produce(f"WB{slot}", sW)

        KSTG = [AA[:, 2 * 8192 + 512 * i:2 * 8192 + 512 * (i + 1)] for i in range(2)]
        VSTG = [AA[:, 2 * 8192 + 1024 + 512 * i:2 * 8192 + 1024 + 512 * (i + 1)]
                for i in range(2)]
        qkv_rot = [0]

        def qkv_group(lhsT_fn, rhs_fn):
            bi = 2 + (qkv_rot[0] % 2)
            qkv_rot[0] += 1
            sch.acquire(t_, f"P{bi}")
            mm = None
            for cc in range(16):
                mm = t_.matmul(P[bi][:, 0:512], lhsT_fn(cc), rhs_fn(cc),
                               start=(cc == 0), stop=(cc == 15))
            return bi, inc(mm, sPE)

        stg_count = 0
        cc_stg_counts = {}
        wslot = [0]

        def next_wslot():
            s = wslot[0]
            wslot[0] ^= 1
            return s

        for g in range(NG):
            kslot = next_wslot()
            load_w_window(C + 512 * g, kslot)
            vslot = next_wslot()
            load_w_window(2 * C + 512 * g, vslot)
            sch.consume(t_, f"WB{kslot}")
            sch.wt(t_, sV, xnf_done)
            wb_last = 0
            for oc in range(4):
                for w2 in range(2):
                    bi, ev = qkv_group(
                        lambda cc, oc=oc, ks=kslot: wb(ks)[:, 512 * cc + 128 * oc:
                                                          512 * cc + 128 * (oc + 1)],
                        lambda cc, w2=w2: AB[:, 1024 * cc + 512 * w2:
                                             1024 * cc + 512 * (w2 + 1)])
                    wb_last = ev
                    st = (oc * 2 + w2) % 2
                    sch.acquire(s_, f"KSTG{st}")
                    sch.wt(s_, sPE, ev)
                    sev = inc(s_.activation(KSTG[st], P[bi][:, 0:512], AF.Identity),
                              sS)
                    sch.note_read(f"P{bi}", sS, sev)
                    sch.wt(y_, sS, sev)
                    d = inc(y_.dma_start(
                        out=kvstg[g].ap()[128 * oc:128 * (oc + 1),
                                          512 * w2:512 * (w2 + 1)],
                        in_=KSTG[st]), sSTG, 16)
                    stg_count += 1
                    sch.note_read(f"KSTG{st}", sSTG, d)
            sch.note_read(f"WB{kslot}", sPE, wb_last)
            sch.consume(t_, f"WB{vslot}")
            vview = kvstg[g].ap()[512:1024, :].rearrange("a (b f) -> (a b) f", b=2)
            for tt in range(8):
                bi, ev = qkv_group(
                    lambda cc, tt=tt: AB[:, 1024 * cc + 128 * tt:
                                         1024 * cc + 128 * (tt + 1)],
                    lambda cc, vs=vslot: wb(vs)[:, 512 * cc:512 * (cc + 1)])
                wb_last = ev
                st = tt % 2
                sch.acquire(v_, f"VSTG{st}")
                sch.wt(v_, sPE, ev)
                vev = inc(v_.tensor_copy(VSTG[st], P[bi][:, 0:512]), sV)
                sch.note_read(f"P{bi}", sV, vev)
                sch.wt(y_, sV, vev)
                d = inc(y_.dma_start(out=vview[128 * tt:128 * (tt + 1), :],
                                     in_=VSTG[st]), sSTG, 16)
                stg_count += 1
                sch.note_read(f"VSTG{st}", sSTG, d)
            sch.note_read(f"WB{vslot}", sPE, wb_last)
            cc_stg_counts[g] = stg_count

        qf_done = 0
        for qw in range(4):
            qslot = 2 + (qw % 2)
            if qw < 2:
                for st in range(2):
                    sch.acquire(s_, f"KSTG{st}")
                    sch.acquire(s_, f"VSTG{st}")
            load_w_window(512 * qw, qslot)
            sch.consume(t_, f"WB{qslot}")
            wb_last = 0
            for oc in range(4):
                h = 4 * qw + oc
                for w2 in range(2):
                    bi, ev = qkv_group(
                        lambda cc, oc=oc, qs=qslot: wb(qs)[:, 512 * cc + 128 * oc:
                                                           512 * cc + 128 * (oc + 1)],
                        lambda cc, w2=w2: AB[:, 1024 * cc + 512 * w2:
                                             1024 * cc + 512 * (w2 + 1)])
                    wb_last = ev
                    sch.wt(s_, sPE, ev)
                    sev = inc(s_.activation(
                        QF[:, 1024 * h + 512 * w2:1024 * h + 512 * (w2 + 1)],
                        P[bi][:, 0:512], AF.Identity), sS)
                    sch.note_read(f"P{bi}", sS, sev)
                    qf_done = sev
            sch.note_read(f"WB{qslot}", sPE, wb_last)
        qkv_pe_done = sPE.n
        g_.wait_ge(sS.h, qf_done)
        for g in range(NG):
            g_.wait_ge(sSTG.h, 16 * cc_stg_counts[g])
            if no_cc or g not in cc_groups:
                g_.dma_start(out=gath[g].ap()[1024 * 0:1024 * 1, :],
                             in_=kvstg[g].ap()).then_inc(sCC[g].h, 16)
            else:
                g_.collective_compute(
                    "AllGather", ALU.bypass,
                    replica_groups=[list(range(W))],
                    ins=[kvstg[g].ap().opt()],
                    outs=[gath[g].ap().opt()],
                ).then_inc(sCC[g].h)
            sCC[g].n = 1

        if max_phase <= 2:
            _dump()
            for g in range(NG):
                y_.wait_ge(sCC[g].h, 1)
            for t in range(NT):
                inc(y_.dma_start(out=out_d[128 * t:128 * (t + 1), :], in_=xin(0)),
                    sOUT, 16)
            y_.wait_ge(sOUT.h, sOUT.n)
            return nc
        env = dict(AA=AA, AB=AB, QF=QF, ATT=ATT, BCS=BCS, RCP=RCP, MASK=MASK,
                   ONESC=ONESC, ONERF=ONERF, MB20=MB20, P=P, gath=gath, sCC=sCC,
                   sPE=sPE, sS=sS, qf_done=qf_done, qkv_pe_done=qkv_pe_done,
                   body_local_src=body_local_src, kvstg=kvstg,
                   body_mode=body_mode, neutral=neutral,
                   body_heads=body_heads, body_konly=body_konly,
                   body_noccw=body_noccw, body_nodma=body_nodma)
        pre_snap = sch.snap()
        if no_switch:
            body_iter = [0]
        else:
            pids = [eng.partition_id() for eng in (t_, v_, s_, y_)]
            body_iter = nc.Switch(engines=[t_, v_, s_, y_], index=pids, n=W)
        for c in body_iter:
            sch.restore(pre_snap)
            body = dict(bKV=Ctr(bKV.h), bSC=Ctr(bSC.h), bEX=Ctr(bEX.h),
                        bMK=Ctr(bMK.h), bPE=Ctr(bPE.h), bHD=Ctr(bHD.h),
                        bVV=Ctr(bVV.h))
            _attention_body(nc, c, sch, body, env)
            inc(t_.nop(), sDT)
            sDT.n = 1
            inc(v_.engine_nop(), sDV)
            sDV.n = 1
            s_.copy(TM1[:, :], TM1[:, :]).then_inc(sDS.h, 1)
            sDS.n = 1
            inc(y_.nop(), sDY)
            sDY.n = 1
        sch.restore(pre_snap)
        for eng in (t_, v_, s_, y_, g_):
            for d in (sDT, sDV, sDS, sDY):
                eng.wait_ge(d.h, 1)
                sch.last[(id(eng), id(d))] = 1
        sch.readers = {}

        if max_phase <= 3:
            _dump()
            for t in range(NT):
                inc(y_.dma_start(out=out_d[128 * t:128 * (t + 1), :], in_=xin(0)),
                    sOUT, 16)
            y_.wait_ge(sOUT.h, sOUT.n)
            return nc
        for cc16 in range(16):
            par = cc16 % 2
            sch.acquire(s_, f"XNT{par}")
            d = inc(s_.dma_start(out=xnt(par),
                                 in_=wpt_d[128 * cc16:128 * (cc16 + 1), :]), sWL, 16)
            sch.produce(f"XNT{par}", sWL, d)
            sch.consume(s_, f"XNT{par}")
            ev = inc(s_.activation(AA[:, 2048 * cc16:2048 * (cc16 + 1)], xnt(par),
                                   AF.Identity), sW)
            sch.note_read(f"XNT{par}", sW, ev)
        pwb0 = sW.n

        for qt in range(NT):
            p = qt % 2
            banks = [0, 1, 2, 3] if p == 0 else [4, 5, 6, 7]
            sch.acquire(y_, f"XIN{p}")
            inc(y_.dma_start(out=xin(p), in_=xin_d[128 * qt:128 * (qt + 1), :]),
                sIN, 16)
            sch.produce(f"XIN{p}", sIN)
            sch.wt(t_, sW, pwb0)
            pev = 0
            for fw in range(4):
                bi = banks[fw]
                sch.acquire(t_, f"P{bi}")
                mm = None
                for cc in range(16):
                    mm = t_.matmul(
                        P[bi][:, 0:512],
                        AB[:, 1024 * cc + 128 * qt:1024 * cc + 128 * (qt + 1)],
                        AA[:, 2048 * cc + 512 * fw:2048 * cc + 512 * (fw + 1)],
                        start=(cc == 0), stop=(cc == 15))
                pev = inc(mm, sPE)
            sch.note_read("AByfm", sPE, pev)
            sch.note_read("PWB", sPE, pev)
            sch.acquire(v_, "S0")
            sch.wt(v_, sPE, pev)
            vev = 0
            for fw in range(4):
                vev = inc(v_.tensor_copy(S0[:, 512 * fw:512 * (fw + 1)],
                                         P[banks[fw]][:, 0:512]), sV)
            for fw in range(4):
                sch.note_read(f"P{banks[fw]}", sV, vev)
            if "proj" in dump:
                sch.wt(y_, sV, vev)
                d = inc(y_.dma_start(out=dbgp.ap()[128 * qt:128 * (qt + 1), :],
                                     in_=S0), sXR, 16)
                sch.note_read("S0", sXR, d)
            sb_ = 48 * p
            sch.acquire(v_, f"STAT{p}")
            mean, var = ln_stats(S0, sb_)
            sch.acquire(s_, "S1")
            ev = ln_apply(S0, S1, sb_, mean, var)
            sch.note_read(f"STAT{p}", sS, ev)
            sch.wt(v_, sS, ev)
            sch.consume(v_, f"XIN{p}")
            vev = inc(v_.tensor_add(S0, S1, xin(p)), sV)
            sch.note_read("S1", sV, vev)
            sch.note_read(f"XIN{p}", sV, vev)
            sch.wt(y_, sV, vev)
            d = inc(y_.dma_start(out=xres.ap()[128 * qt:128 * (qt + 1), :], in_=S0),
                    sXR, 16)
            sch.note_read("S0", sXR, d)

        if max_phase <= 4:
            _dump()
            for t in range(NT):
                y_.wait_ge(sXR.h, 16 * NT)
                inc(y_.dma_start(out=out_d[128 * t:128 * (t + 1), :], in_=xin(0)),
                    sOUT, 16)
            y_.wait_ge(sOUT.h, sOUT.n)
            return nc
        xqf_tile_ev = {}
        for qt in range(NT):
            p = qt % 2
            sch.acquire(y_, f"XIN{p}")
            sch.wt(y_, sXR, (32 if "proj" in dump else 16) * (qt + 1))
            inc(y_.dma_start(out=xin(p), in_=xres.ap()[128 * qt:128 * (qt + 1), :]),
                sIN, 16)
            sch.produce(f"XIN{p}", sIN)
            sch.consume(v_, f"XIN{p}")
            sb_ = 48 * p
            sch.acquire(v_, f"STAT{p}")
            mean, var = ln_stats(xin(p), sb_)
            sch.acquire(s_, "S0")
            ev = ln_apply(xin(p), S0, sb_, mean, var)
            sch.note_read(f"XIN{p}", sS, ev)
            sch.note_read(f"STAT{p}", sS, ev)
            sch.wt(v_, sS, ev)
            rc127, scb = quant_stats(S0, 96 + 16 * p)
            sch.acquire(v_, "S1")
            ev = inc(v_.tensor_scalar(S1, S0, rc127, CMG[:, :], ALU.mult, ALU.add),
                     sV)
            sch.note_read("S0", sV, ev)
            sch.acquire(v_, f"XNT{p}")
            ev = inc(v_.tensor_scalar(xnt(p), S1, CMG[:, :], scb,
                                      ALU.subtract, ALU.mult), sV)
            sch.note_read("S1", sV, ev)
            sch.produce(f"XNT{p}", sV, ev)
            sch.consume(t_, f"XNT{p}")
            sch.acquire(t_, "AByfm") if qt == 0 else None
            for cc in range(16):
                bk = P[cc % 4]
                sch.acquire(t_, f"P{cc % 4}")
                tev = inc(t_.transpose(bk[:, 0:128],
                                       xnt(p)[:, 128 * cc:128 * (cc + 1)],
                                       IDN[:, :]), sPE)
                if cc == 15:
                    sch.note_read(f"XNT{p}", sPE, tev)
                sch.wt(v_, sPE, tev)
                ev = inc(v_.tensor_copy(
                    AB[:, 1024 * cc + 128 * qt:1024 * cc + 128 * (qt + 1)],
                    bk[:, 0:128]), sV)
                sch.note_read(f"P{cc % 4}", sV, ev)
            xqf_tile_ev[qt] = ev

        if max_phase <= 5:
            _dump()
            for t in range(NT):
                inc(y_.dma_start(out=out_d[128 * t:128 * (t + 1), :], in_=xin(0)),
                    sOUT, 16)
            y_.wait_ge(sOUT.h, sOUT.n)
            return nc
        KQT = [AA[:, 2 * 8192 + 2048 * i:2 * 8192 + 2048 * (i + 1)] for i in range(2)]
        sch.acquire(y_, "PWB")
        for tp in range(4):
            tts = (2 * tp, 2 * tp + 1)
            for tt in tts:
                sch.wt(t_, sV, xqf_tile_ev[tt])
            for bi in range(8):
                sch.acquire(t_, f"P{bi}")
            for dc in range(16):
                kp = dc % 2
                sch.acquire(y_, f"KQT{kp}")
                sch.wt(y_, sKQD, 16 * 16)
                if no_dmat:
                    d = inc(y_.dma_start(
                        out=KQT[kp], in_=kqd.ap()[128 * dc:128 * (dc + 1), :].rearrange(
                            "a (b c) -> a b c", b=1)), sKQ2, 16)
                else:
                    d = inc(y_.dma_start_transpose(
                        KQT[kp], kqd.ap()[:, 128 * dc:128 * (dc + 1)]), sKQ2, 16)
                sch.produce(f"KQT{kp}", sKQ2, d)
                sch.consume(t_, f"KQT{kp}")
                mm = None
                for ti, tt in enumerate(tts):
                    banks = [0, 1, 2, 3] if ti == 0 else [4, 5, 6, 7]
                    for fw in range(4):
                        mm = t_.matmul(
                            P[banks[fw]][:, 0:512],
                            AB[:, 1024 * dc + 128 * tt:1024 * dc + 128 * (tt + 1)],
                            KQT[kp][:, 512 * fw:512 * (fw + 1)],
                            start=(dc == 0), stop=False)
                kev = inc(mm, sPE)
                sch.note_read(f"KQT{kp}", sPE, kev)
            sch.consume(t_, "BQ")
            for ti, tt in enumerate(tts):
                banks = [0, 1, 2, 3] if ti == 0 else [4, 5, 6, 7]
                mm = None
                for fw in range(4):
                    mm = t_.matmul(P[banks[fw]][:, 0:512], ONERB[:, :],
                                   BQ[0:1, 512 * fw:512 * (fw + 1)],
                                   start=False, stop=True)
                pev = inc(mm, sPE)
                p = tt % 2
                sch.acquire(y_, f"XIN{p}")
                inc(y_.dma_start(out=xin(p),
                                 in_=xres.ap()[128 * tt:128 * (tt + 1), :]), sIN, 16)
                sch.produce(f"XIN{p}", sIN)
                sch.acquire(v_, "S0")
                sch.wt(v_, sPE, pev)
                vev = 0
                for fw in range(4):
                    vev = inc(v_.tensor_scalar_max(S0[:, 512 * fw:512 * (fw + 1)],
                                                   P[banks[fw]][:, 0:512], 0.0), sV)
                for fw in range(4):
                    sch.note_read(f"P{banks[fw]}", sV, vev)
                rc127, scb = quant_stats(S0, 96 + 16 * p)
                sch.acquire(v_, "S1")
                v_.tensor_scalar(S1, S0, rc127, CMG[:, :], ALU.mult, ALU.add)
                v_.tensor_scalar(S0, S1, CMG[:, :], scb, ALU.subtract, ALU.mult)
                sb_ = 48 * p
                sch.acquire(v_, f"STAT{p}")
                mean, var = ln_stats(S0, sb_)
                ev = ln_apply(S0, S1, sb_, mean, var)
                sch.note_read(f"STAT{p}", sS, ev)
                sch.wt(v_, sS, ev)
                sch.consume(v_, f"XIN{p}")
                vev = inc(v_.tensor_add(S0, S1, xin(p)), sV)
                sch.note_read("S1", sV, vev)
                sch.note_read(f"XIN{p}", sV, vev)
                sch.wt(y_, sV, vev)
                d = inc(y_.dma_start(out=out_d[128 * tt:128 * (tt + 1), :], in_=S0),
                        sOUT, 16)
                sch.note_read("S0", sOUT, d)
        _dump()
        y_.wait_ge(sOUT.h, sOUT.n)
    return nc


def _attention_body(nc, c, sch, body, env):
    t_, v_, s_, y_ = nc.tensor, nc.vector, nc.scalar, nc.sync
    AA, AB, QF, ATT, BCS, RCP = (env["AA"], env["AB"], env["QF"], env["ATT"],
                                 env["BCS"], env["RCP"])
    MASK, ONESC, ONERF, MB20 = env["MASK"], env["ONESC"], env["ONERF"], env["MB20"]
    P = env["P"]
    gath = env["gath"]
    sCC, sPE, sS = env["sCC"], env["sPE"], env["sS"]
    bKV, bSC, bEX, bMK, bPE, bHD, bVV = (body["bKV"], body["bSC"], body["bEX"],
                                         body["bMK"], body["bPE"], body["bHD"],
                                         body["bVV"])

    def KT(par):
        return AA[:, 8192 * par:8192 * (par + 1)]

    def VT(par):
        return AA[:, 8192 * (2 + par):8192 * (3 + par)]

    nblk = 16 - c
    sch.wt(t_, sS, env["qf_done"])
    sch.wt(y_, sPE, env["qkv_pe_done"])
    att_last_av = [0, 0, 0, 0]

    for h in range(env.get('body_heads', NH)):
        par = h % 2
        g = h // 4
        hh = h % 4
        if not env.get("body_noccw"):
            sch.wt(y_, sCC[g], 1)
        if h >= 2:
            sch.wt(y_, bHD, h - 1)
        for b in range(nblk):
            r, loc = _owner_loc(b)
            if env.get("body_local_src") == "kvstg":
                srcbase = env["kvstg"][g].ap()
                r = 0
            elif env.get("body_local_src") == "neutral":
                srcbase = env["neutral"].ap()
                r = r % 2
            else:
                srcbase = gath[g].ap()
            src = srcbase[1024 * r + 128 * hh:1024 * r + 128 * (hh + 1),
                          loc:loc + 512]
            if not env.get("body_nodma"):
                inc(y_.dma_start(out=KT(par)[:, 512 * b:512 * (b + 1)], in_=src),
                    bKV, 16)
            vv = srcbase[1024 * r + 512:1024 * r + 1024, :] \
                .rearrange("a (b f) -> (a b) f", b=2)
            src2 = vv[loc:loc + 512, 128 * hh:128 * (hh + 1)] \
                .rearrange("(j p) d -> p j d", p=128)
            dst2 = VT(par)[:, 512 * b:512 * (b + 1)].rearrange("p (j d) -> p j d",
                                                               d=128)
            if not env.get("body_konly"):
                inc(y_.dma_start(out=dst2, in_=src2), bKV, 16)
        kv_ev = bKV.n
        if env.get("body_mode") == "loads":
            sch.wt(y_, bKV, kv_ev)
            inc(y_.nop(), bHD)
            continue
        hd_mark = None
        for w in range(2):
            bw = c if w == 0 else 15 - c
            n = 4 * bw + 4
            qcol = 1024 * h + 512 * w
            av_b = 4 + w
            dn_b = 6 + w
            sch.wt(t_, bKV, kv_ev)
            scev = {}
            exev = {}
            avev = {}
            mkev = {}

            def emit_sc(ci):
                sl = ci % 4
                bk = P[sl]
                sch.acquire(t_, f"P{sl}")
                scev[ci] = inc(t_.matmul(bk[:, 0:512],
                                         KT(par)[:, 128 * ci:128 * (ci + 1)],
                                         QF[:, qcol:qcol + 512],
                                         start=True, stop=True), bSC)
                dpos = ci - (n - 4)
                if dpos >= 0:
                    sch.wt(v_, bSC, scev[ci])
                    mkev[ci] = inc(v_.tensor_add(
                        bk[:, 0:512], bk[:, 0:512],
                        MASK[:, 512 * dpos:512 * (dpos + 1)]), bMK)
                sch.wt(s_, bSC, scev[ci])
                if ci in mkev:
                    sch.wt(s_, bMK, mkev[ci])
                if att_last_av[sl]:
                    sch.wt(s_, bPE, att_last_av[sl])
                exev[ci] = inc(s_.activation(ATT[:, 512 * sl:512 * (sl + 1)],
                                             bk[:, 0:512], AF.Exp,
                                             bias=MB20[:, :], scale=SCALE), bEX)
                sch.note_read(f"P{sl}", bEX, exev[ci])

            def emit_av(ci):
                sl = ci % 4
                if ci == 0:
                    sch.acquire(t_, f"P{av_b}")
                    sch.acquire(t_, f"P{dn_b}")
                sch.wt(t_, bEX, exev[ci])
                t_.matmul(P[av_b][:, 0:512], VT(par)[:, 128 * ci:128 * (ci + 1)],
                          ATT[:, 512 * sl:512 * (sl + 1)],
                          start=(ci == 0), stop=(ci == n - 1))
                avev[ci] = inc(t_.matmul(P[dn_b][0:1, 0:512], ONESC[:, :],
                                         ATT[:, 512 * sl:512 * (sl + 1)],
                                         start=(ci == 0), stop=(ci == n - 1)), bPE)
                att_last_av[sl] = avev[ci]

            emit_sc(0)
            for ci in range(1, n):
                emit_sc(ci)
                emit_av(ci - 1)
            emit_av(n - 1)
            if w == 1:
                hd_mark = t_.nop()
            sch.wt(v_, bPE, avev[n - 1])
            sch.acquire(v_, "RCP")
            v_.reciprocal(RCP[:, :], P[dn_b][0:1, 0:512])
            rcev = inc(v_.drain(), bVV)
            sch.note_read(f"P{dn_b}", bVV, rcev)
            sch.produce("RCP", bVV, rcev)
            bsl = n % 4
            sch.acquire(t_, f"P{bsl}")
            sch.consume(t_, "RCP")
            bcev = inc(t_.matmul(P[bsl][:, 0:512], ONERF[:, :], RCP[:, :],
                                 start=True, stop=True), bPE)
            sch.wt(v_, bPE, bcev)
            sch.acquire(v_, "BCS")
            bev = inc(v_.tensor_copy(BCS[:, :], P[bsl][:, 0:512]), bVV)
            sch.note_read(f"P{bsl}", bVV, bev)
            sch.note_read("RCP", bVV, bev)
            v_.drain()
            if h == 0 and w == 0:
                sch.wt(v_, sPE, env["qkv_pe_done"])
            yev = inc(v_.tensor_mul(AB[:, qcol:qcol + 512], P[av_b][:, 0:512],
                                    BCS[:, :]), bVV)
            sch.note_read(f"P{av_b}", bVV, yev)
            sch.note_read("BCS", bVV, yev)
        inc(hd_mark, bHD)


def kernel(L0_attn_x, L0_attn_w_attn, L0_attn_w_proj, L0_ffn_kernel, L0_ffn_bias):
    x = np.ascontiguousarray(np.asarray(L0_attn_x, dtype=np.float32).reshape(T, C))
    w_attnT = np.ascontiguousarray(np.asarray(L0_attn_w_attn, dtype=np.float32).T)
    w_projT = np.ascontiguousarray(np.asarray(L0_attn_w_proj, dtype=np.float32).T)
    kernelT = np.ascontiguousarray(np.asarray(L0_ffn_kernel, dtype=np.float32).T)
    bias = np.ascontiguousarray(np.asarray(L0_ffn_bias, dtype=np.float32).reshape(1, C))

    in_maps = []
    for c in range(W):
        bA, bB = _own_blocks(c)
        xs = np.concatenate([x[512 * bA:512 * (bA + 1)],
                             x[512 * bB:512 * (bB + 1)]], axis=0)
        in_maps.append({
            "x_sh": np.ascontiguousarray(xs),
            "w_attnT": w_attnT,
            "w_projT": w_projT,
            "kernelT": kernelT,
            "bias": bias,
        })

    nc = build()
    res = run_bass_kernel_spmd(nc, in_maps, core_ids=list(range(W)))
    out = np.empty((T, C), dtype=np.float32)
    for c in range(W):
        sh = np.asarray(res.results[c]["out_sh"])
        bA, bB = _own_blocks(c)
        out[512 * bA:512 * (bA + 1)] = sh[:512]
        out[512 * bB:512 * (bB + 1)] = sh[512:]
    return out.reshape(1, T, C)
